# version 14
# baseline (speedup 1.0000x reference)
import numpy as np
import ml_dtypes
from contextlib import ExitStack

import concourse.bass as bass
import concourse.mybir as mybir
from concourse.bass_utils import run_bass_kernel_spmd

F32 = mybir.dt.float32
BF16 = mybir.dt.bfloat16
AF = mybir.ActivationFunctionType
ALU = mybir.AluOpType
AX = mybir.AxisListType

NCORES = 8
BPC = 2
S = 2048
D = 1024
NBLK = 4
TB = 512
NTIL = 4
CSHIFT = 24.0
EPS = 1e-5

ENGS = ("pe", "act", "dve", "pool", "sp")
NDMALANES = 8
DOMS = (("pe", "act", "dve", "pool")
        + tuple(f"dsp{l}" for l in range(NDMALANES))
        + tuple(f"dpool{l}" for l in range(NDMALANES)))
f32 = np.float32


class Sched:
    def __init__(self):
        self.q = {e: [] for e in ENGS}
        self.cnt = {d: 0 for d in DOMS}
        self.clock = {e: {d: 0 for d in DOMS} for e in ENGS}
        self.snaps = {}
        self.res = {}
        self.dma_i = {"dsp": 0, "dpool": 0}

    def op(self, eng, fn, reads=(), writes=(), dma=False):
        if dma:
            qn = {"sp": "dsp", "pool": "dpool"}[eng]
            dom = f"{qn}{self.dma_i[qn] % NDMALANES}"
            self.dma_i[qn] += 1
        else:
            dom = eng
        need = {}

        self_sync = dma or dom in ("dve", "act", "pool")

        def add(d, c):
            if (d != dom or self_sync) and c > need.get(d, 0):
                need[d] = c

        for k in reads:
            r = self.res.get(k)
            if r and r["w"]:
                add(*r["w"])
        for k in writes:
            r = self.res.get(k)
            if r:
                if r["w"]:
                    add(*r["w"])
                for d, c in r["r"].items():
                    add(d, c)
        clk = self.clock[eng]
        if self_sync and need.get(dom, 0) >= self.cnt[dom]:
            pass
        waits = sorted((d, c) for d, c in need.items() if c > clk[d])
        for d, c in waits:
            if c > clk[d]:
                clk[d] = c
            snap = self.snaps.get((d, c))
            if snap:
                for d2, c2 in snap.items():
                    if c2 > clk[d2]:
                        clk[d2] = c2
        inc = 16 if dma else 1
        self.cnt[dom] += inc
        cdone = self.cnt[dom]
        snap = dict(clk)
        snap[dom] = cdone
        self.snaps[(dom, cdone)] = snap
        self.q[eng].append((waits, fn, dom, inc))
        for k in writes:
            self.res[k] = {"w": (dom, cdone), "r": {}}
        for k in reads:
            r = self.res.setdefault(k, {"w": None, "r": {}})
            if r["r"].get(dom, 0) < cdone:
                r["r"][dom] = cdone
        return cdone

    def raw_wait(self, eng, dom, val):
        self.q[eng].append(([(dom, val)], None, None, 0))


def build_nc():
    nc = bass.Bass("TRN2", target_bir_lowering=False)

    x = nc.dram_tensor("x", [BPC * S, D], F32, kind="ExternalInput")
    wt_in = {n: nc.dram_tensor(f"w{n}T", [D, D], F32, kind="ExternalInput")
             for n in "qkvof"}
    bvr = nc.dram_tensor("bvr", [1, D], F32, kind="ExternalInput")
    bor = nc.dram_tensor("bor", [1, D], F32, kind="ExternalInput")
    bfr = nc.dram_tensor("bfr", [1, D], F32, kind="ExternalInput")
    qbc_in = nc.dram_tensor("qbc", [128, 8], F32, kind="ExternalInput")
    kbc_in = nc.dram_tensor("kbc", [128, 8], F32, kind="ExternalInput")
    alv_in = nc.dram_tensor("alv", [128, 1], F32, kind="ExternalInput")
    ident = nc.dram_tensor("ident", [128, 128], BF16, kind="ExternalInput")
    y = nc.dram_tensor("y", [BPC * S, D], F32, kind="ExternalOutput")
    xnd = nc.dram_tensor("xnd", [S, D], BF16, kind="Internal")

    with ExitStack() as ctx:
        e = ctx.enter_context
        wT = {n: e(nc.sbuf_tensor(f"wT_{n}", [128, 8 * 1024], BF16))
              for n in "qkvof"}
        k_fm = e(nc.sbuf_tensor("k_fm", [128, 8 * 2048], BF16))
        v_aug = e(nc.sbuf_tensor("v_aug", [128, 16 * 1040], BF16))
        xn_fm = e(nc.sbuf_tensor("xn_fm", [128, 2 * 8 * 512], BF16))
        q_sb = e(nc.sbuf_tensor("q_sb", [128, 2 * 512], BF16))
        expT = e(nc.sbuf_tensor("expT", [128, 2 * 1024], BF16))
        ctxn = e(nc.sbuf_tensor("ctxn", [128, 512], BF16))
        ctx_fm = e(nc.sbuf_tensor("ctx_fm", [128, 2 * 8 * 512], BF16))
        g_sb = e(nc.sbuf_tensor("g_sb", [128, 2 * 1024], F32))
        xt = e(nc.sbuf_tensor("xt", [128, 1024], F32))
        xnt = e(nc.sbuf_tensor("xnt", [128, 1024], BF16))
        brow = e(nc.sbuf_tensor("brow", [1, 2 * 1024], BF16))
        ones_bf = e(nc.sbuf_tensor("ones_bf", [1, 128], BF16))
        qb_c = e(nc.sbuf_tensor("qb_c", [128, 8], F32))
        kb_c = e(nc.sbuf_tensor("kb_c", [128, 8], F32))
        al_v = e(nc.sbuf_tensor("al_v", [128, 1], F32))
        ndel = e(nc.sbuf_tensor("ndel", [128, 1], F32))
        id_sb = e(nc.sbuf_tensor("id_sb", [128, 128], BF16))
        ones1 = e(nc.sbuf_tensor("ones1", [1, 128], F32))
        onec = e(nc.sbuf_tensor("onec", [128, 1], F32))
        stats = e(nc.sbuf_tensor("stats", [128, 12], F32))
        aggr = e(nc.sbuf_tensor("aggr", [128, 2], F32))
        scol = e(nc.sbuf_tensor("scol", [128, 4], F32))
        absd = e(nc.sbuf_tensor("absd", [128, 9], F32))
        dsc = e(nc.sbuf_tensor("dsc", [1, 1], F32))
        ps_s = e(nc.psum_tensor("ps_s", [128, 2048], F32))
        ps_c = e(nc.psum_tensor("ps_c", [128, 1024], F32))
        ps_t = e(nc.psum_tensor("ps_t", [128, 512], F32))
        ps_p = e(nc.psum_tensor("ps_p", [128, 512], F32))

        def tern_stage(sl):
            return ctx_fm[:, sl * 2048:(sl + 1) * 2048].bitcast(F32)

        sems = {d: e(nc.semaphore(f"sem_{d}")) for d in DOMS}
        block = e(nc.Block())
        S_ = Sched()

        def bank(i, c0=0, c1=512):
            if i < 4:
                return ps_s[:, i * 512 + c0:i * 512 + c1], f"ps{i}"
            if i < 6:
                return ps_c[:, (i - 4) * 512 + c0:(i - 4) * 512 + c1], f"ps{i}"
            if i == 6:
                return ps_t[:, c0:c1], "ps6"
            return ps_p[:, c0:c1], "ps7"


        S_.op("dve", lambda v: v.memset(ones1[:, :], 1.0), writes=["ones1"])
        S_.op("dve", lambda v: v.memset(ones_bf[:, :], 1.0), writes=["ones_bf"])
        S_.op("pool", lambda g: g.dma_start(brow[:, 1024:2048], bfr[:, :]),
              writes=["brow2"], dma=True)
        S_.op("dve", lambda v: v.memset(onec[:, :], 1.0), writes=["onec"])
        for tt in range(16):
            ap = bass.AP(v_aug, tt * 1040 + 64, [[16 * 1040, 128], [65, 16], [1, 1]])
            S_.op("dve", lambda v, ap=ap: v.memset(ap, 1.0),
                  writes=[f"v{tt}_0", f"v{tt}_1"])
        S_.op("pool", lambda g: g.dma_start(id_sb[:, :], ident[:, :]),
              writes=["id_sb"], dma=True)
        S_.op("pool", lambda g: g.dma_start(qb_c[:, :], qbc_in[:, :]),
              writes=["qb_c"], dma=True)
        S_.op("pool", lambda g: g.dma_start(kb_c[:, :], kbc_in[:, :]),
              writes=["kb_c"], dma=True)
        S_.op("pool", lambda g: g.dma_start(al_v[:, :], alv_in[:, :]),
              writes=["al_v"], dma=True)

        for n in ("k", "v"):
            for kc in range(8):
                S_.op("pool", lambda g, n=n, kc=kc: g.dma_start(
                    wT[n][:, kc * 1024:(kc + 1) * 1024],
                    wt_in[n][kc * 128:(kc + 1) * 128, :]),
                    writes=[f"wT_{n}{kc}"], dma=True)

        def setup_late():
            for n in ("q", "o"):
                for kc in range(8):
                    S_.op("pool", lambda g, n=n, kc=kc: g.dma_start(
                        wT[n][:, kc * 1024:(kc + 1) * 1024],
                        wt_in[n][kc * 128:(kc + 1) * 128, :]),
                        writes=[f"wT_{n}{kc}"], dma=True)
            setup_ffn()

        def setup_ffn():
            for kc in range(8):
                sl = kc % 2
                stg = tern_stage(sl)
                skey = f"tern{sl}"
                S_.op("pool", lambda g, kc=kc, stg=stg: g.dma_start(
                    stg, wt_in["f"][kc * 128:(kc + 1) * 128, :]),
                    writes=[skey], dma=True)
                S_.op("dve", lambda v, kc=kc, stg=stg: v.reduce_sum(
                    absd[:, kc:kc + 1], stg, AX.X, apply_absolute_value=True),
                    reads=[skey], writes=[f"absd{kc}"])
            S_.op("dve", lambda v: v.reduce_sum(absd[:, 8:9], absd[:, 0:8], AX.X),
                  reads=[f"absd{k}" for k in range(8)], writes=["absd8"])
            pk7 = "ps7"
            S_.op("pe", lambda t: t.matmul(ps_p[0:1, 0:1], absd[:, 8:9], onec[:, :],
                                           start=True, stop=True),
                  reads=["absd8", "onec"], writes=[pk7])
            S_.op("act", lambda s: s.mul(dsc[:, :], ps_p[0:1, 0:1],
                                         f32(0.7 / (D * D))),
                  reads=[pk7], writes=["dsc"])
            S_.op("pe", lambda t: t.matmul(ps_p[:, 0:1], ones1[:, :], dsc[:, :],
                                           start=True, stop=True),
                  reads=["dsc", "ones1"], writes=[pk7])
            S_.op("dve", lambda v: v.tensor_scalar(ndel[:, :], ps_p[:, 0:1], -1.0,
                                                   None, ALU.mult),
                  reads=[pk7], writes=["ndel"])
            for kc in range(8):
                sl = kc % 2
                stg = tern_stage(sl)
                skey = f"tern{sl}"
                S_.op("pool", lambda g, kc=kc, stg=stg: g.dma_start(
                    stg, wt_in["f"][kc * 128:(kc + 1) * 128, :]),
                    writes=[skey], dma=True)
                gat = g_sb[:, sl * 1024:(sl + 1) * 1024]
                gkey = f"gat{sl}"
                sgn = ctx_fm[:, 4096:6144].bitcast(F32)
                dkey = "sgn0"
                S_.op("act", lambda s, stg=stg, sgn=sgn: s.activation(
                    sgn, stg, AF.Abs, bias=f32(0.0), scale=f32(1.0)),
                    reads=[skey], writes=[dkey])
                S_.op("dve", lambda v, gat=gat, sgn=sgn: v.tensor_scalar(
                    gat, sgn, ndel[:, 0:1], None, ALU.add),
                    reads=[dkey, "ndel"], writes=[gkey])
                S_.op("act", lambda s, gat=gat, sgn=sgn: s.activation(
                    sgn, gat, AF.Sign, bias=f32(0.0), scale=f32(1.0)),
                    reads=[gkey], writes=[dkey])
                S_.op("dve", lambda v, gat=gat, sgn=sgn: v.tensor_relu(gat, sgn),
                      reads=[dkey], writes=[gkey])
                S_.op("act", lambda s, stg=stg, sgn=sgn: s.activation(
                    sgn, stg, AF.Sign, bias=f32(0.0), scale=f32(1.0)),
                    reads=[skey], writes=[dkey])
                S_.op("dve", lambda v, gat=gat, sgn=sgn: v.tensor_tensor(
                    gat, gat, sgn, ALU.mult),
                    reads=[gkey, dkey], writes=[gkey])
                S_.op("dve", lambda v, kc=kc, gat=gat: v.tensor_scalar(
                    wT["f"][:, kc * 1024:(kc + 1) * 1024], gat, al_v[:, 0:1],
                    None, ALU.mult),
                    reads=[gkey, "al_v"], writes=[f"wT_f{kc}"])

        wT_keys = {n: [f"wT_{n}{kc}" for kc in range(8)] for n in "qkvof"}

        fin_pending = []

        def flush_fin():
            while fin_pending:
                fin_pending.pop(0)()

        def ln_tile(b, tt, base):
            t0 = base + b * TB + tt * 128
            sl = tt % 2
            xti = xt[:, :]
            xkey = "xt0"
            S_.op("pool", lambda g, t0=t0, xti=xti: g.dma_start(
                xti, x[t0:t0 + 128, :]), writes=[xkey], dma=True)
            S_.op("dve", lambda v, xti=xti: v.bn_stats(
                stats[:, 0:6], xti[:, 0:512]), reads=[xkey], writes=["st1"])
            S_.op("dve", lambda v, xti=xti: v.bn_stats(
                stats[:, 6:12], xti[:, 512:1024]), reads=[xkey], writes=["st2"])
            S_.op("dve", lambda v: v.bn_aggr(aggr[:, 0:2], stats[:, 0:12]),
                  reads=["st1", "st2"], writes=["aggr"])
            S_.op("act", lambda s: s.activation(
                scol[:, 0:1], aggr[:, 1:2], AF.Sqrt, bias=f32(EPS),
                scale=f32(1.0)), reads=["aggr"], writes=["std"])
            S_.op("dve", lambda v: v.reciprocal(scol[:, 1:2], scol[:, 0:1]),
                  reads=["std"], writes=["rstd"])
            sl = tt % 2
            xnti = xnt[:, :]
            nkey = "xnt0"
            S_.op("dve", lambda v, xti=xti, xnti=xnti, xkey=xkey: v.tensor_scalar(
                xnti, xti, aggr[:, 0:1], scol[:, 1:2],
                ALU.subtract, ALU.mult),
                reads=[xkey, "aggr", "rstd"], writes=[nkey, "sgn0"])
            S_.op("sp", lambda sy, t0=t0, base=base, xnti=xnti: sy.dma_start(
                xnd[t0 - base:t0 - base + 128, :], xnti),
                reads=[nkey], writes=[f"xnd{b}_{tt}"], dma=True)

        def ln_front(b, tt, base):
            t0 = base + b * TB + tt * 128
            sl = tt % 2
            xti = xt[:, :]
            xkey = "xt0"
            S_.op("pool", lambda g, t0=t0, xti=xti: g.dma_start(
                xti, x[t0:t0 + 128, :]), writes=[xkey], dma=True)
            S_.op("dve", lambda v, xti=xti: v.bn_stats(
                stats[:, 0:6], xti[:, 0:512]), reads=[xkey], writes=["st1"])
            S_.op("dve", lambda v, xti=xti: v.bn_stats(
                stats[:, 6:12], xti[:, 512:1024]), reads=[xkey], writes=["st2"])
            S_.op("dve", lambda v: v.bn_aggr(aggr[:, 0:2], stats[:, 0:12]),
                  reads=["st1", "st2"], writes=["aggr"])

        def ln_finish(b, tt, base):
            t0 = base + b * TB + tt * 128
            sl = tt % 2
            xti = xt[:, :]
            xkey = "xt0"
            S_.op("act", lambda s: s.activation(
                scol[:, 0:1], aggr[:, 1:2], AF.Sqrt, bias=f32(EPS),
                scale=f32(1.0)), reads=["aggr"], writes=["std"])
            S_.op("dve", lambda v: v.reciprocal(scol[:, 1:2], scol[:, 0:1]),
                  reads=["std"], writes=["rstd"])
            xnti = xnt[:, :]
            nkey = "xnt0"
            S_.op("dve", lambda v, xti=xti, xnti=xnti, xkey=xkey: v.tensor_scalar(
                xnti, xti, aggr[:, 0:1], scol[:, 1:2],
                ALU.subtract, ALU.mult),
                reads=[xkey, "aggr", "rstd"], writes=[nkey, "sgn0"])
            S_.op("sp", lambda sy, t0=t0, base=base, xnti=xnti: sy.dma_start(
                xnd[t0 - base:t0 - base + 128, :], xnti),
                reads=[nkey], writes=[f"xnd{b}_{tt}"], dma=True)

        def xn_loads(b):
            pb = b % 2
            xo = pb * 4096
            for kc in range(8):
                S_.op("sp", lambda sy, b=b, kc=kc, xo=xo: sy.dma_start(
                    xn_fm[:, xo + kc * 512:xo + (kc + 1) * 512],
                    xnd[b * TB:(b + 1) * TB, kc * 128:(kc + 1) * 128],
                    transpose=True),
                    reads=[f"xnd{b}_{t}" for t in range(NTIL)],
                    writes=[f"xnfm{pb}_{kc}"], dma=True)

        def kv_block(b):
            pb = b % 2
            xo = pb * 4096
            for oc in range(8):
                ap, pk = bank(oc % 2)
                for kc in range(8):
                    S_.op("pe", lambda t, oc=oc, kc=kc, ap=ap, xo=xo: t.matmul(
                        ap,
                        wT["k"][:, kc * 1024 + oc * 128:kc * 1024 + oc * 128 + 128],
                        xn_fm[:, xo + kc * 512:xo + (kc + 1) * 512],
                        start=(kc == 0), stop=(kc == 7)),
                        reads=[wT_keys["k"][kc], f"xnfm{pb}_{kc}"], writes=[pk])
                S_.op("dve", lambda v, oc=oc, b=b, ap=ap: v.tensor_scalar(
                    k_fm[:, oc * 2048 + b * 512:oc * 2048 + b * 512 + 512],
                    ap, kb_c[:, oc:oc + 1], None, ALU.add),
                    reads=[pk, "kb_c"], writes=[f"k{oc}_{b}"])
            for tt in range(NTIL):
                gt = b * NTIL + tt
                for hf in range(2):
                    ap, pk = bank(2 + (tt * 2 + hf) % 2)
                    for kc in range(8):
                        S_.op("pe", lambda t, tt=tt, hf=hf, kc=kc, ap=ap, xo=xo: t.matmul(
                            ap,
                            xn_fm[:, xo + kc * 512 + tt * 128:xo + kc * 512 + tt * 128 + 128],
                            wT["v"][:, kc * 1024 + hf * 512:kc * 1024 + hf * 512 + 512],
                            start=(kc == 0), stop=False),
                            reads=[wT_keys["v"][kc], f"xnfm{pb}_{kc}"], writes=[pk])
                    S_.op("pe", lambda t, hf=hf, ap=ap: t.matmul(
                        ap, ones_bf[:, :], brow[:, hf * 512:(hf + 1) * 512],
                        start=False, stop=True),
                        reads=["ones_bf", "brow0"], writes=[pk])
                    vap = bass.AP(v_aug, gt * 1040 + hf * 8 * 65,
                                  [[16 * 1040, 128], [65, 8], [1, 64]])
                    S_.op("dve", lambda v, vap=vap, ap=ap: v.tensor_copy(
                        vap, ap.rearrange("p (h e) -> p h e", e=64)),
                        reads=[pk], writes=[f"v{gt}_{hf}"])

        def q_proj(b, hp2):
            pb = b % 2
            xo = pb * 4096
            qsl2 = hp2 % 2
            qap2 = q_sb[:, qsl2 * 512:(qsl2 + 1) * 512]
            ap7, pk7 = bank(7)
            for kc in range(8):
                S_.op("pe", lambda t, hp2=hp2, kc=kc, ap7=ap7, xo=xo: t.matmul(
                    ap7,
                    wT["q"][:, kc * 1024 + hp2 * 128:kc * 1024 + hp2 * 128 + 128],
                    xn_fm[:, xo + kc * 512:xo + (kc + 1) * 512],
                    start=(kc == 0), stop=(kc == 7)),
                    reads=[wT_keys["q"][kc], f"xnfm{pb}_{kc}"], writes=[pk7])
            S_.op("dve", lambda v, hp2=hp2, qap2=qap2, ap7=ap7: v.tensor_scalar(
                qap2, ap7, qb_c[:, hp2:hp2 + 1], None, ALU.add),
                reads=[pk7, "qb_c"], writes=[f"q{qsl2}"])

        def attn_pair(b, hp, q_next):
            cpo = (b % 2) * 4096
            """scores/exp/AV pipeline for head pair hp; q_next() emitted at i==9;
            finisher (norm+transpose+evac) deferred into the NEXT pair."""
            qsl = hp % 2
            qapA = q_sb[0:64, qsl * 512:(qsl + 1) * 512]
            qapB = q_sb[64:128, qsl * 512:(qsl + 1) * 512]
            qkey = f"q{qsl}"
            kkeys = [f"k{hp}_{bb}" for bb in range(NBLK)]

            def scores(i):
                p = i % 2
                sA, skA = bank(2 * p)
                sB, skB = bank(2 * p + 1)
                S_.op("pe", lambda t, i=i, sA=sA: t.matmul(
                    sA, k_fm[0:64, hp * 2048 + i * 128:hp * 2048 + i * 128 + 128],
                    qapA, start=True, stop=True, tile_position=(0, 0)),
                    reads=[kkeys[i // 4], qkey], writes=[skA])
                S_.op("pe", lambda t, i=i, sB=sB: t.matmul(
                    sB, k_fm[64:128, hp * 2048 + i * 128:hp * 2048 + i * 128 + 128],
                    qapB, start=True, stop=True, tile_position=(64, 0)),
                    reads=[kkeys[i // 4], qkey], writes=[skB])
                eap = expT[:, p * 1024:(p + 1) * 1024]
                S_.op("act", lambda s, p=p, eap=eap: s.activation(
                    eap, ps_s[:, p * 1024:(p + 1) * 1024], AF.Exp,
                    bias=f32(-CSHIFT), scale=f32(0.125)),
                    reads=[skA, skB], writes=[f"exp{p}"])

            def cslot(h, j):
                return bank(4 + j // 2, h * 130 + (j % 2) * 65,
                            h * 130 + (j % 2) * 65 + 65)

            def av(i):
                p = i % 2
                eap = expT[:, p * 1024:(p + 1) * 1024]
                for j in range(4):
                    cA, ckA = cslot(0, j)
                    S_.op("pe", lambda t, i=i, j=j, eap=eap, cA=cA: t.matmul(
                        cA, eap[:, j * 128:j * 128 + 128],
                        v_aug[:, i * 1040 + 2 * hp * 65:i * 1040 + 2 * hp * 65 + 65],
                        start=(i == 0 and j % 2 == 0), stop=(i == 15 and j == 3)),
                        reads=[f"exp{p}", f"v{i}_{(2 * hp) // 8}"], writes=[ckA])
                    cB, ckB = cslot(1, j)
                    S_.op("pe", lambda t, i=i, j=j, eap=eap, cB=cB: t.matmul(
                        cB, eap[:, 512 + j * 128:512 + j * 128 + 128],
                        v_aug[:, i * 1040 + (2 * hp + 1) * 65:i * 1040 + (2 * hp + 1) * 65 + 65],
                        start=False, stop=(i == 15 and j == 3)),
                        reads=[f"exp{p}", f"v{i}_{(2 * hp + 1) // 8}"], writes=[ckB])

            def norm_j(i, j):
                bk = 4 + j // 2
                c0 = (j % 2) * 65
                cAr, ckA = bank(bk, c0 + 64, c0 + 65)
                cAv, _ = bank(bk, c0, c0 + 64)
                cBr, ckB = bank(bk, 130 + c0 + 64, 130 + c0 + 65)
                cBv, _ = bank(bk, 130 + c0, 130 + c0 + 64)
                S_.op("dve", lambda v, cAr=cAr: v.reciprocal(scol[:, 2:3], cAr),
                      reads=[ckA], writes=["rA"])
                S_.op("dve", lambda v, j=j, cAv=cAv: v.tensor_scalar(
                    ctxn[:, j * 128:j * 128 + 64], cAv,
                    scol[:, 2:3], None, ALU.mult),
                    reads=[ckA, "rA"], writes=[f"cn{j}"])
                S_.op("dve", lambda v, cBr=cBr: v.reciprocal(scol[:, 3:4], cBr),
                      reads=[ckB], writes=["rB"])
                S_.op("dve", lambda v, j=j, cBv=cBv: v.tensor_scalar(
                    ctxn[:, j * 128 + 64:j * 128 + 128], cBv,
                    scol[:, 3:4], None, ALU.mult),
                    reads=[ckB, "rB"], writes=[f"cn{j}"])

            def av15(hp=hp):
                p = 15 % 2
                eap = expT[:, p * 1024:(p + 1) * 1024]
                for j in range(4):
                    cA, ckA = cslot(0, j)
                    S_.op("pe", lambda t, j=j, eap=eap, cA=cA: t.matmul(
                        cA, eap[:, j * 128:j * 128 + 128],
                        v_aug[:, 15 * 1040 + 2 * hp * 65:15 * 1040 + 2 * hp * 65 + 65],
                        start=False, stop=True),
                        reads=[f"exp{p}", f"v15_{(2 * hp) // 8}"], writes=[ckA])
                    cB, ckB = cslot(1, j)
                    S_.op("pe", lambda t, j=j, eap=eap, cB=cB: t.matmul(
                        cB, eap[:, 512 + j * 128:512 + j * 128 + 128],
                        v_aug[:, 15 * 1040 + (2 * hp + 1) * 65:15 * 1040 + (2 * hp + 1) * 65 + 65],
                        start=False, stop=True),
                        reads=[f"exp{p}", f"v15_{(2 * hp + 1) // 8}"], writes=[ckB])
                    if j == 1:
                        norm_j(15, 0)
                        norm_j(15, 1)
                norm_j(15, 2)
                norm_j(15, 3)

            scores(0)
            scores(1)
            flush_fin()
            q_next()
            av(0)
            for i in range(2, 16):
                scores(i)
                av(i - 1)
            av15()

            def fin(hp=hp, cpo=cpo, pb=b % 2):
                for j in range(4):
                    tv = ps_t[:, j * 64:j * 64 + 64].bitcast(BF16)
                    S_.op("pe", lambda t, j=j, tv=tv: t.transpose(
                        tv, ctxn[:, j * 128:j * 128 + 128], id_sb[:, :]),
                        reads=[f"cn{j}", "id_sb"], writes=["ps6"])
                ev = ps_t[:, 0:256].bitcast(BF16)
                S_.op("dve", lambda v, hp=hp, ev=ev, cpo=cpo: v.tensor_copy(
                    ctx_fm[:, cpo + hp * 512:cpo + (hp + 1) * 512], ev),
                    reads=["ps6"],
                    writes=([f"cf{pb}_{hp}", f"tern{hp // 4}"]
                            + (["sgn0"] if (pb == 1 and hp < 4) else [])))

            fin_pending.append(fin)

        def b35_tile(b, tt, base):
            pb = b % 2
            xo = pb * 4096
            cpo = pb * 4096
            t0 = base + b * TB + tt * 128
            sl = tt % 2
            xti = xt[:, :]
            xkey = "xt0"
            S_.op("pool", lambda g, t0=t0, xti=xti: g.dma_start(
                xti, x[t0:t0 + 128, :]), writes=[xkey], dma=True)
            gsl = g_sb[:, sl * 1024:(sl + 1) * 1024]
            gkey2 = f"g{sl}"
            for hf in range(2):
                gp, gk = bank(7)
                for kc in range(8):
                    S_.op("pe", lambda t, tt=tt, hf=hf, kc=kc, gp=gp, xo=xo: t.matmul(
                        gp,
                        xn_fm[:, xo + kc * 512 + tt * 128:xo + kc * 512 + tt * 128 + 128],
                        wT["f"][:, kc * 1024 + hf * 512:kc * 1024 + hf * 512 + 512],
                        start=(kc == 0), stop=False),
                        reads=[wT_keys["f"][kc], f"xnfm{pb}_{kc}"], writes=[gk])
                S_.op("pe", lambda t, hf=hf, gp=gp: t.matmul(
                    gp, ones_bf[:, :], brow[:, 1024 + hf * 512:1024 + hf * 512 + 512],
                    start=False, stop=True),
                    reads=["ones_bf", "brow2"], writes=[gk])
                S_.op("act", lambda s, hf=hf, gsl=gsl, gp=gp: s.activation(
                    gsl[:, hf * 512:(hf + 1) * 512], gp, AF.Silu,
                    bias=f32(0.0), scale=f32(1.0)),
                    reads=[gk], writes=[gkey2, "gat0", "gat1"])
                sp_, sk = bank(6)
                for kc in range(8):
                    S_.op("pe", lambda t, tt=tt, hf=hf, kc=kc, sp_=sp_, cpo=cpo: t.matmul(
                        sp_,
                        ctx_fm[:, cpo + kc * 512 + tt * 128:cpo + kc * 512 + tt * 128 + 128],
                        wT["o"][:, kc * 1024 + hf * 512:kc * 1024 + hf * 512 + 512],
                        start=(kc == 0), stop=False),
                        reads=[wT_keys["o"][kc], f"cf{pb}_{kc}"], writes=[sk])
                S_.op("pe", lambda t, hf=hf, sp_=sp_: t.matmul(
                    sp_, ones_bf[:, :], brow[:, hf * 512:hf * 512 + 512],
                    start=False, stop=True),
                    reads=["ones_bf", "brow0"], writes=[sk])
                S_.op("dve", lambda v, hf=hf, gsl=gsl, sp_=sp_: v.tensor_tensor(
                    gsl[:, hf * 512:(hf + 1) * 512], sp_,
                    gsl[:, hf * 512:(hf + 1) * 512], ALU.mult),
                    reads=[sk, gkey2], writes=[gkey2])
            S_.op("dve", lambda v, gsl=gsl, xti=xti, xkey=xkey: v.tensor_tensor(
                gsl, gsl, xti, ALU.add),
                reads=[gkey2, xkey], writes=[gkey2])
            S_.op("sp", lambda sy, t0=t0, gsl=gsl: sy.dma_start(
                y[t0:t0 + 128, :], gsl),
                reads=[gkey2], writes=[f"y{t0}"], dma=True)

        for sq in range(BPC):
            base = sq * S
            S_.op("pool", lambda g: g.dma_start(brow[:, 0:1024], bvr[:, :]),
                  writes=["brow0"], dma=True)
            for b in range(NBLK):
                if sq == 0:
                    for tt in range(NTIL):
                        ln_tile(b, tt, base)
                    if b == 0:
                        setup_late()
                xn_loads(b)
                if b >= 1:
                    kv_block(b - 1)
            kv_block(NBLK - 1)
            S_.op("pool", lambda g: g.dma_start(brow[:, 0:1024], bor[:, :]),
                  writes=["brow0"], dma=True)
            for b in range(NBLK):
                if b == 0:
                    xn_loads(0)
                    q_proj(0, 0)
                b35_sched = {1: 0, 3: 1, 5: 2, 6: 3}
                for hp in range(8):
                    if hp < 7:
                        qn = (lambda b=b, hp=hp: q_proj(b, hp + 1))
                    elif b + 1 < NBLK:
                        qn = (lambda b=b: q_proj(b + 1, 0))
                    else:
                        qn = (lambda: None)
                    attn_pair(b, hp, qn)
                    if sq + 1 < BPC and 0 <= hp - 3 <= 3:
                        ln_finish(b, hp - 3, (sq + 1) * S)
                    if b >= 1 and hp in b35_sched:
                        b35_tile(b - 1, b35_sched[hp], base)
                    if sq + 1 < BPC and 0 <= hp - 2 <= 3:
                        ln_front(b, hp - 2, (sq + 1) * S)
                    if hp == 6 and b + 1 < NBLK:
                        xn_loads(b + 1)
                flush_fin()
            for tt in range(NTIL):
                b35_tile(NBLK - 1, tt, base)

        for l in range(NDMALANES):
            if S_.cnt[f"dsp{l}"]:
                S_.raw_wait("sp", f"dsp{l}", S_.cnt[f"dsp{l}"])

        def replay(eng, obj):
            for waits, fn, dom, inc in S_.q[eng]:
                for d, c in waits:
                    obj.wait_ge(sems[d], c)
                if fn is not None:
                    fn(obj).then_inc(sems[dom], inc)

        @block.tensor
        def _(t):
            replay("pe", t)

        @block.scalar
        def _(s):
            replay("act", s)

        @block.vector
        def _(v):
            replay("dve", v)

        @block.gpsimd
        def _(g):
            replay("pool", g)

        @block.sync
        def _(sy):
            replay("sp", sy)

    return nc


_NC_CACHE = None


def kernel(L0_attn_query, L0_attn_mask, L0_attn_q_w, L0_attn_q_b, L0_attn_k_w,
           L0_attn_k_b, L0_attn_v_w, L0_attn_v_b, L0_attn_out_w, L0_attn_out_b,
           L0_ffn_weight, L0_ffn_alpha, L0_ffn_bias):
    global _NC_CACHE
    if _NC_CACHE is None:
        _NC_CACHE = build_nc()
    nc = _NC_CACHE

    xq = np.ascontiguousarray(np.asarray(L0_attn_query, dtype=np.float32))
    trp = lambda w: np.ascontiguousarray(np.asarray(w, dtype=np.float32).T)
    row = lambda v: np.ascontiguousarray(
        np.asarray(v, dtype=np.float32).reshape(1, -1))
    colc = lambda v: np.ascontiguousarray(
        np.asarray(v, dtype=np.float32).reshape(8, 128).T)
    common = {
        "wqT": trp(L0_attn_q_w), "wkT": trp(L0_attn_k_w), "wvT": trp(L0_attn_v_w),
        "woT": trp(L0_attn_out_w), "wfT": trp(L0_ffn_weight),
        "bvr": row(L0_attn_v_b), "bor": row(L0_attn_out_b), "bfr": row(L0_ffn_bias),
        "qbc": colc(L0_attn_q_b), "kbc": colc(L0_attn_k_b),
        "alv": np.ascontiguousarray(np.broadcast_to(
            np.asarray(L0_ffn_alpha, dtype=np.float32).reshape(1, 1), (128, 1))),
        "ident": np.eye(128, dtype=np.float32).astype(ml_dtypes.bfloat16),
    }
    in_maps = []
    for c in range(NCORES):
        m = dict(common)
        m["x"] = np.ascontiguousarray(
            xq[c * BPC:(c + 1) * BPC].reshape(BPC * S, D))
        in_maps.append(m)
    res = run_bass_kernel_spmd(nc, in_maps, core_ids=list(range(NCORES)))
    out = np.concatenate(
        [np.asarray(r["y"]).reshape(BPC, S, D) for r in res.results], axis=0)
    return out.astype(np.float32, copy=False)


# version 15
# speedup vs baseline: 1.2429x; 1.2429x over previous
import numpy as np
import ml_dtypes
from contextlib import ExitStack

import concourse.bass as bass
import concourse.mybir as mybir
from concourse.bass_utils import run_bass_kernel_spmd

F32 = mybir.dt.float32
BF16 = mybir.dt.bfloat16
AF = mybir.ActivationFunctionType
ALU = mybir.AluOpType
AX = mybir.AxisListType

NCORES = 8
BPC = 2
S = 2048
D = 1024
NBLK = 4
TB = 512
NTIL = 4
CSHIFT = 24.0
EPS = 1e-5

ENGS = ("pe", "act", "dve", "pool", "sp")
NDMALANES = 8
DOMS = (("pe", "act", "dve", "pool")
        + tuple(f"dsp{l}" for l in range(NDMALANES))
        + tuple(f"dpool{l}" for l in range(NDMALANES)))
f32 = np.float32


class Sched:
    def __init__(self):
        self.q = {e: [] for e in ENGS}
        self.cnt = {d: 0 for d in DOMS}
        self.clock = {e: {d: 0 for d in DOMS} for e in ENGS}
        self.snaps = {}
        self.res = {}
        self.dma_i = {"dsp": 0, "dpool": 0}

    def op(self, eng, fn, reads=(), writes=(), dma=False):
        if dma:
            qn = {"sp": "dsp", "pool": "dpool"}[eng]
            dom = f"{qn}{self.dma_i[qn] % NDMALANES}"
            self.dma_i[qn] += 1
        else:
            dom = eng
        need = {}

        self_sync = dma or dom in ("dve", "act", "pool")

        def add(d, c):
            if (d != dom or self_sync) and c > need.get(d, 0):
                need[d] = c

        for k in reads:
            r = self.res.get(k)
            if r and r["w"]:
                add(*r["w"])
        for k in writes:
            r = self.res.get(k)
            if r:
                if r["w"]:
                    add(*r["w"])
                for d, c in r["r"].items():
                    add(d, c)
        clk = self.clock[eng]
        if self_sync and need.get(dom, 0) >= self.cnt[dom]:
            pass
        waits = sorted((d, c) for d, c in need.items() if c > clk[d])
        for d, c in waits:
            if c > clk[d]:
                clk[d] = c
            snap = self.snaps.get((d, c))
            if snap:
                for d2, c2 in snap.items():
                    if c2 > clk[d2]:
                        clk[d2] = c2
        inc = 16 if dma else 1
        self.cnt[dom] += inc
        cdone = self.cnt[dom]
        snap = dict(clk)
        snap[dom] = cdone
        self.snaps[(dom, cdone)] = snap
        self.q[eng].append((waits, fn, dom, inc))
        for k in writes:
            self.res[k] = {"w": (dom, cdone), "r": {}}
        for k in reads:
            r = self.res.setdefault(k, {"w": None, "r": {}})
            if r["r"].get(dom, 0) < cdone:
                r["r"][dom] = cdone
        return cdone

    def raw_wait(self, eng, dom, val):
        self.q[eng].append(([(dom, val)], None, None, 0))


def build_nc():
    nc = bass.Bass("TRN2", target_bir_lowering=False)

    x = nc.dram_tensor("x", [BPC * S, D], F32, kind="ExternalInput")
    wt_in = {n: nc.dram_tensor(f"w{n}T", [D, D], F32, kind="ExternalInput")
             for n in "qkvof"}
    bvr = nc.dram_tensor("bvr", [1, D], F32, kind="ExternalInput")
    bor = nc.dram_tensor("bor", [1, D], F32, kind="ExternalInput")
    bfr = nc.dram_tensor("bfr", [1, D], F32, kind="ExternalInput")
    qbc_in = nc.dram_tensor("qbc", [128, 8], F32, kind="ExternalInput")
    kbc_in = nc.dram_tensor("kbc", [128, 8], F32, kind="ExternalInput")
    alv_in = nc.dram_tensor("alv", [128, 1], F32, kind="ExternalInput")
    ident = nc.dram_tensor("ident", [128, 128], BF16, kind="ExternalInput")
    y = nc.dram_tensor("y", [BPC * S, D], F32, kind="ExternalOutput")
    xnd = nc.dram_tensor("xnd", [S, D], BF16, kind="Internal")

    with ExitStack() as ctx:
        e = ctx.enter_context
        wT = {n: e(nc.sbuf_tensor(f"wT_{n}", [128, 8 * 1024], BF16))
              for n in "qkvof"}
        k_fm = e(nc.sbuf_tensor("k_fm", [128, 8 * 2048], BF16))
        v_aug = e(nc.sbuf_tensor("v_aug", [128, 16 * 1040], BF16))
        xn_fm = e(nc.sbuf_tensor("xn_fm", [128, 2 * 8 * 512], BF16))
        q_sb = e(nc.sbuf_tensor("q_sb", [128, 2 * 512], BF16))
        expT = e(nc.sbuf_tensor("expT", [128, 2 * 1024], BF16))
        ctxn = e(nc.sbuf_tensor("ctxn", [128, 512], BF16))
        ctx_fm = e(nc.sbuf_tensor("ctx_fm", [128, 2 * 8 * 512], BF16))
        g_sb = e(nc.sbuf_tensor("g_sb", [128, 2 * 1024], F32))
        xt = e(nc.sbuf_tensor("xt", [128, 1024], F32))
        xnt = e(nc.sbuf_tensor("xnt", [128, 1024], BF16))
        brow = e(nc.sbuf_tensor("brow", [1, 2 * 1024], BF16))
        ones_bf = e(nc.sbuf_tensor("ones_bf", [1, 128], BF16))
        qb_c = e(nc.sbuf_tensor("qb_c", [128, 8], F32))
        kb_c = e(nc.sbuf_tensor("kb_c", [128, 8], F32))
        al_v = e(nc.sbuf_tensor("al_v", [128, 1], F32))
        ndel = e(nc.sbuf_tensor("ndel", [128, 1], F32))
        id_sb = e(nc.sbuf_tensor("id_sb", [128, 128], BF16))
        ones1 = e(nc.sbuf_tensor("ones1", [1, 128], F32))
        onec = e(nc.sbuf_tensor("onec", [128, 1], F32))
        stats = e(nc.sbuf_tensor("stats", [128, 12], F32))
        aggr = e(nc.sbuf_tensor("aggr", [128, 2], F32))
        scol = e(nc.sbuf_tensor("scol", [128, 4], F32))
        absd = e(nc.sbuf_tensor("absd", [128, 9], F32))
        dsc = e(nc.sbuf_tensor("dsc", [1, 1], F32))
        ps_s = e(nc.psum_tensor("ps_s", [128, 2048], F32))
        ps_c = e(nc.psum_tensor("ps_c", [128, 1024], F32))
        ps_t = e(nc.psum_tensor("ps_t", [128, 512], F32))
        ps_p = e(nc.psum_tensor("ps_p", [128, 512], F32))

        def tern_stage(sl):
            return ctx_fm[:, sl * 2048:(sl + 1) * 2048].bitcast(F32)

        sems = {d: e(nc.semaphore(f"sem_{d}")) for d in DOMS}
        block = e(nc.Block())
        S_ = Sched()

        def bank(i, c0=0, c1=512):
            if i < 4:
                return ps_s[:, i * 512 + c0:i * 512 + c1], f"ps{i}"
            if i < 6:
                return ps_c[:, (i - 4) * 512 + c0:(i - 4) * 512 + c1], f"ps{i}"
            if i == 6:
                return ps_t[:, c0:c1], "ps6"
            return ps_p[:, c0:c1], "ps7"


        S_.op("dve", lambda v: v.memset(ones1[:, :], 1.0), writes=["ones1"])
        S_.op("dve", lambda v: v.memset(ones_bf[:, :], 1.0), writes=["ones_bf"])
        S_.op("pool", lambda g: g.dma_start(brow[:, 1024:2048], bfr[:, :]),
              writes=["brow2"], dma=True)
        S_.op("dve", lambda v: v.memset(onec[:, :], 1.0), writes=["onec"])
        for tt in range(16):
            ap = bass.AP(v_aug, tt * 1040 + 64, [[16 * 1040, 128], [65, 16], [1, 1]])
            S_.op("dve", lambda v, ap=ap: v.memset(ap, 1.0),
                  writes=[f"v{tt}_0", f"v{tt}_1"])
        S_.op("pool", lambda g: g.dma_start(id_sb[:, :], ident[:, :]),
              writes=["id_sb"], dma=True)
        S_.op("pool", lambda g: g.dma_start(qb_c[:, :], qbc_in[:, :]),
              writes=["qb_c"], dma=True)
        S_.op("pool", lambda g: g.dma_start(kb_c[:, :], kbc_in[:, :]),
              writes=["kb_c"], dma=True)
        S_.op("pool", lambda g: g.dma_start(al_v[:, :], alv_in[:, :]),
              writes=["al_v"], dma=True)

        for n in ("k", "v"):
            for kc in range(8):
                S_.op("pool", lambda g, n=n, kc=kc: g.dma_start(
                    wT[n][:, kc * 1024:(kc + 1) * 1024],
                    wt_in[n][kc * 128:(kc + 1) * 128, :]),
                    writes=[f"wT_{n}{kc}"], dma=True)

        def setup_late():
            for n in ("q", "o"):
                for kc in range(8):
                    S_.op("pool", lambda g, n=n, kc=kc: g.dma_start(
                        wT[n][:, kc * 1024:(kc + 1) * 1024],
                        wt_in[n][kc * 128:(kc + 1) * 128, :]),
                        writes=[f"wT_{n}{kc}"], dma=True)
            setup_ffn()

        def setup_ffn():
            for kc in range(8):
                sl = kc % 2
                stg = tern_stage(sl)
                skey = f"tern{sl}"
                S_.op("pool", lambda g, kc=kc, stg=stg: g.dma_start(
                    stg, wt_in["f"][kc * 128:(kc + 1) * 128, :]),
                    writes=[skey], dma=True)
                S_.op("dve", lambda v, kc=kc, stg=stg: v.reduce_sum(
                    absd[:, kc:kc + 1], stg, AX.X, apply_absolute_value=True),
                    reads=[skey], writes=[f"absd{kc}"])
            S_.op("dve", lambda v: v.reduce_sum(absd[:, 8:9], absd[:, 0:8], AX.X),
                  reads=[f"absd{k}" for k in range(8)], writes=["absd8"])
            pk7 = "ps7"
            S_.op("pe", lambda t: t.matmul(ps_p[0:1, 0:1], absd[:, 8:9], onec[:, :],
                                           start=True, stop=True),
                  reads=["absd8", "onec"], writes=[pk7])
            S_.op("act", lambda s: s.mul(dsc[:, :], ps_p[0:1, 0:1],
                                         f32(0.7 / (D * D))),
                  reads=[pk7], writes=["dsc"])
            S_.op("pe", lambda t: t.matmul(ps_p[:, 0:1], ones1[:, :], dsc[:, :],
                                           start=True, stop=True),
                  reads=["dsc", "ones1"], writes=[pk7])
            S_.op("dve", lambda v: v.tensor_scalar(ndel[:, :], ps_p[:, 0:1], -1.0,
                                                   None, ALU.mult),
                  reads=[pk7], writes=["ndel"])
            for kc in range(8):
                sl = kc % 2
                stg = tern_stage(sl)
                skey = f"tern{sl}"
                S_.op("pool", lambda g, kc=kc, stg=stg: g.dma_start(
                    stg, wt_in["f"][kc * 128:(kc + 1) * 128, :]),
                    writes=[skey], dma=True)
                gat = g_sb[:, sl * 1024:(sl + 1) * 1024]
                gkey = f"gat{sl}"
                sgn = ctx_fm[:, 4096:6144].bitcast(F32)
                dkey = "sgn0"
                S_.op("act", lambda s, stg=stg, sgn=sgn: s.activation(
                    sgn, stg, AF.Abs, bias=f32(0.0), scale=f32(1.0)),
                    reads=[skey], writes=[dkey])
                S_.op("dve", lambda v, gat=gat, sgn=sgn: v.tensor_scalar(
                    gat, sgn, ndel[:, 0:1], None, ALU.add),
                    reads=[dkey, "ndel"], writes=[gkey])
                S_.op("act", lambda s, gat=gat, sgn=sgn: s.activation(
                    sgn, gat, AF.Sign, bias=f32(0.0), scale=f32(1.0)),
                    reads=[gkey], writes=[dkey])
                S_.op("dve", lambda v, gat=gat, sgn=sgn: v.tensor_relu(gat, sgn),
                      reads=[dkey], writes=[gkey])
                S_.op("act", lambda s, stg=stg, sgn=sgn: s.activation(
                    sgn, stg, AF.Sign, bias=f32(0.0), scale=f32(1.0)),
                    reads=[skey], writes=[dkey])
                S_.op("dve", lambda v, gat=gat, sgn=sgn: v.tensor_tensor(
                    gat, gat, sgn, ALU.mult),
                    reads=[gkey, dkey], writes=[gkey])
                S_.op("dve", lambda v, kc=kc, gat=gat: v.tensor_scalar(
                    wT["f"][:, kc * 1024:(kc + 1) * 1024], gat, al_v[:, 0:1],
                    None, ALU.mult),
                    reads=[gkey, "al_v"], writes=[f"wT_f{kc}"])

        wT_keys = {n: [f"wT_{n}{kc}" for kc in range(8)] for n in "qkvof"}

        fin_pending = []

        def flush_fin():
            while fin_pending:
                fin_pending.pop(0)()

        def ln_tile(b, tt, base):
            t0 = base + b * TB + tt * 128
            sl = tt % 2
            xti = xt[:, :]
            xkey = "xt0"
            S_.op("pool", lambda g, t0=t0, xti=xti: g.dma_start(
                xti, x[t0:t0 + 128, :]), writes=[xkey], dma=True)
            S_.op("dve", lambda v, xti=xti: v.bn_stats(
                stats[:, 0:6], xti[:, 0:512]), reads=[xkey], writes=["st1"])
            S_.op("dve", lambda v, xti=xti: v.bn_stats(
                stats[:, 6:12], xti[:, 512:1024]), reads=[xkey], writes=["st2"])
            S_.op("dve", lambda v: v.bn_aggr(aggr[:, 0:2], stats[:, 0:12]),
                  reads=["st1", "st2"], writes=["aggr"])
            S_.op("act", lambda s: s.activation(
                scol[:, 0:1], aggr[:, 1:2], AF.Sqrt, bias=f32(EPS),
                scale=f32(1.0)), reads=["aggr"], writes=["std"])
            S_.op("dve", lambda v: v.reciprocal(scol[:, 1:2], scol[:, 0:1]),
                  reads=["std"], writes=["rstd"])
            sl = tt % 2
            xnti = xnt[:, :]
            nkey = "xnt0"
            S_.op("dve", lambda v, xti=xti, xnti=xnti, xkey=xkey: v.tensor_scalar(
                xnti, xti, aggr[:, 0:1], scol[:, 1:2],
                ALU.subtract, ALU.mult),
                reads=[xkey, "aggr", "rstd"], writes=[nkey, "sgn0"])
            S_.op("sp", lambda sy, t0=t0, base=base, xnti=xnti: sy.dma_start(
                xnd[t0 - base:t0 - base + 128, :], xnti),
                reads=[nkey], writes=[f"xnd{b}_{tt}"], dma=True)

        def ln_front(b, tt, base):
            t0 = base + b * TB + tt * 128
            sl = tt % 2
            xti = xt[:, :]
            xkey = "xt0"
            S_.op("pool", lambda g, t0=t0, xti=xti: g.dma_start(
                xti, x[t0:t0 + 128, :]), writes=[xkey], dma=True)
            S_.op("dve", lambda v, xti=xti: v.bn_stats(
                stats[:, 0:6], xti[:, 0:512]), reads=[xkey], writes=["st1"])
            S_.op("dve", lambda v, xti=xti: v.bn_stats(
                stats[:, 6:12], xti[:, 512:1024]), reads=[xkey], writes=["st2"])
            S_.op("dve", lambda v: v.bn_aggr(aggr[:, 0:2], stats[:, 0:12]),
                  reads=["st1", "st2"], writes=["aggr"])

        def ln_finish(b, tt, base):
            t0 = base + b * TB + tt * 128
            sl = tt % 2
            xti = xt[:, :]
            xkey = "xt0"
            S_.op("act", lambda s: s.activation(
                scol[:, 0:1], aggr[:, 1:2], AF.Sqrt, bias=f32(EPS),
                scale=f32(1.0)), reads=["aggr"], writes=["std"])
            S_.op("dve", lambda v: v.reciprocal(scol[:, 1:2], scol[:, 0:1]),
                  reads=["std"], writes=["rstd"])
            xnti = xnt[:, :]
            nkey = "xnt0"
            S_.op("dve", lambda v, xti=xti, xnti=xnti, xkey=xkey: v.tensor_scalar(
                xnti, xti, aggr[:, 0:1], scol[:, 1:2],
                ALU.subtract, ALU.mult),
                reads=[xkey, "aggr", "rstd"], writes=[nkey, "sgn0"])
            S_.op("sp", lambda sy, t0=t0, base=base, xnti=xnti: sy.dma_start(
                xnd[t0 - base:t0 - base + 128, :], xnti),
                reads=[nkey], writes=[f"xnd{b}_{tt}"], dma=True)

        def xn_loads(b):
            pb = b % 2
            xo = pb * 4096
            for kc in range(8):
                S_.op("sp", lambda sy, b=b, kc=kc, xo=xo: sy.dma_start(
                    xn_fm[:, xo + kc * 512:xo + (kc + 1) * 512],
                    xnd[b * TB:(b + 1) * TB, kc * 128:(kc + 1) * 128],
                    transpose=True),
                    reads=[f"xnd{b}_{t}" for t in range(NTIL)],
                    writes=[f"xnfm{pb}_{kc}"], dma=True)

        def kv_block(b):
            pb = b % 2
            xo = pb * 4096
            for oc in range(8):
                ap, pk = bank(oc % 2)
                for kc in range(8):
                    S_.op("pe", lambda t, oc=oc, kc=kc, ap=ap, xo=xo: t.matmul(
                        ap,
                        wT["k"][:, kc * 1024 + oc * 128:kc * 1024 + oc * 128 + 128],
                        xn_fm[:, xo + kc * 512:xo + (kc + 1) * 512],
                        start=(kc == 0), stop=(kc == 7)),
                        reads=[wT_keys["k"][kc], f"xnfm{pb}_{kc}"], writes=[pk])
                S_.op("dve", lambda v, oc=oc, b=b, ap=ap: v.tensor_scalar(
                    k_fm[:, oc * 2048 + b * 512:oc * 2048 + b * 512 + 512],
                    ap, kb_c[:, oc:oc + 1], None, ALU.add),
                    reads=[pk, "kb_c"], writes=[f"k{oc}_{b}"])
            for tt in range(NTIL):
                gt = b * NTIL + tt
                for hf in range(2):
                    ap, pk = bank(2 + (tt * 2 + hf) % 2)
                    for kc in range(8):
                        S_.op("pe", lambda t, tt=tt, hf=hf, kc=kc, ap=ap, xo=xo: t.matmul(
                            ap,
                            xn_fm[:, xo + kc * 512 + tt * 128:xo + kc * 512 + tt * 128 + 128],
                            wT["v"][:, kc * 1024 + hf * 512:kc * 1024 + hf * 512 + 512],
                            start=(kc == 0), stop=False),
                            reads=[wT_keys["v"][kc], f"xnfm{pb}_{kc}"], writes=[pk])
                    S_.op("pe", lambda t, hf=hf, ap=ap: t.matmul(
                        ap, ones_bf[:, :], brow[:, hf * 512:(hf + 1) * 512],
                        start=False, stop=True),
                        reads=["ones_bf", "brow0"], writes=[pk])
                    vap = bass.AP(v_aug, gt * 1040 + hf * 8 * 65,
                                  [[16 * 1040, 128], [65, 8], [1, 64]])
                    S_.op("dve", lambda v, vap=vap, ap=ap: v.tensor_copy(
                        vap, ap.rearrange("p (h e) -> p h e", e=64)),
                        reads=[pk], writes=[f"v{gt}_{hf}"])

        def q_proj(b, hp2):
            pb = b % 2
            xo = pb * 4096
            qsl2 = hp2 % 2
            qap2 = q_sb[:, qsl2 * 512:(qsl2 + 1) * 512]
            ap7, pk7 = bank(7)
            for kc in range(8):
                S_.op("pe", lambda t, hp2=hp2, kc=kc, ap7=ap7, xo=xo: t.matmul(
                    ap7,
                    wT["q"][:, kc * 1024 + hp2 * 128:kc * 1024 + hp2 * 128 + 128],
                    xn_fm[:, xo + kc * 512:xo + (kc + 1) * 512],
                    start=(kc == 0), stop=(kc == 7)),
                    reads=[wT_keys["q"][kc], f"xnfm{pb}_{kc}"], writes=[pk7])
            S_.op("dve", lambda v, hp2=hp2, qap2=qap2, ap7=ap7: v.tensor_scalar(
                qap2, ap7, qb_c[:, hp2:hp2 + 1], None, ALU.add),
                reads=[pk7, "qb_c"], writes=[f"q{qsl2}"])

        def attn_pair(b, hp, q_next):
            cpo = (b % 2) * 4096
            """scores/exp/AV pipeline for head pair hp; q_next() emitted at i==9;
            finisher (norm+transpose+evac) deferred into the NEXT pair."""
            qsl = hp % 2
            qapA = q_sb[0:64, qsl * 512:(qsl + 1) * 512]
            qapB = q_sb[64:128, qsl * 512:(qsl + 1) * 512]
            qkey = f"q{qsl}"
            kkeys = [f"k{hp}_{bb}" for bb in range(NBLK)]

            def scores(i):
                p = i % 2
                sA, skA = bank(2 * p)
                sB, skB = bank(2 * p + 1)
                S_.op("pe", lambda t, i=i, sA=sA: t.matmul(
                    sA, k_fm[0:64, hp * 2048 + i * 128:hp * 2048 + i * 128 + 128],
                    qapA, start=True, stop=True, tile_position=(0, 0)),
                    reads=[kkeys[i // 4], qkey], writes=[skA])
                S_.op("pe", lambda t, i=i, sB=sB: t.matmul(
                    sB, k_fm[64:128, hp * 2048 + i * 128:hp * 2048 + i * 128 + 128],
                    qapB, start=True, stop=True, tile_position=(64, 0)),
                    reads=[kkeys[i // 4], qkey], writes=[skB])
                eap = expT[:, p * 1024:(p + 1) * 1024]
                S_.op("act", lambda s, p=p, eap=eap: s.activation(
                    eap, ps_s[:, p * 1024:(p + 1) * 1024], AF.Exp,
                    bias=f32(-CSHIFT), scale=f32(0.125)),
                    reads=[skA, skB], writes=[f"exp{p}"])

            def cslot(h, j):
                return bank(4 + j // 2, h * 130 + (j % 2) * 65,
                            h * 130 + (j % 2) * 65 + 65)

            def av(i):
                p = i % 2
                eap = expT[:, p * 1024:(p + 1) * 1024]
                for j in range(4):
                    cA, ckA = cslot(0, j)
                    S_.op("pe", lambda t, i=i, j=j, eap=eap, cA=cA: t.matmul(
                        cA, eap[:, j * 128:j * 128 + 128],
                        v_aug[:, i * 1040 + 2 * hp * 65:i * 1040 + 2 * hp * 65 + 65],
                        start=(i == 0 and j % 2 == 0), stop=(i == 15 and j == 3)),
                        reads=[f"exp{p}", f"v{i}_{(2 * hp) // 8}"], writes=[ckA])
                    cB, ckB = cslot(1, j)
                    S_.op("pe", lambda t, i=i, j=j, eap=eap, cB=cB: t.matmul(
                        cB, eap[:, 512 + j * 128:512 + j * 128 + 128],
                        v_aug[:, i * 1040 + (2 * hp + 1) * 65:i * 1040 + (2 * hp + 1) * 65 + 65],
                        start=False, stop=(i == 15 and j == 3)),
                        reads=[f"exp{p}", f"v{i}_{(2 * hp + 1) // 8}"], writes=[ckB])

            def norm_j(i, j):
                bk = 4 + j // 2
                c0 = (j % 2) * 65
                cAr, ckA = bank(bk, c0 + 64, c0 + 65)
                cAv, _ = bank(bk, c0, c0 + 64)
                cBr, ckB = bank(bk, 130 + c0 + 64, 130 + c0 + 65)
                cBv, _ = bank(bk, 130 + c0, 130 + c0 + 64)
                S_.op("dve", lambda v, cAr=cAr: v.reciprocal(scol[:, 2:3], cAr),
                      reads=[ckA], writes=["rA"])
                S_.op("dve", lambda v, j=j, cAv=cAv: v.tensor_scalar(
                    ctxn[:, j * 128:j * 128 + 64], cAv,
                    scol[:, 2:3], None, ALU.mult),
                    reads=[ckA, "rA"], writes=[f"cn{j}"])
                S_.op("dve", lambda v, cBr=cBr: v.reciprocal(scol[:, 3:4], cBr),
                      reads=[ckB], writes=["rB"])
                S_.op("dve", lambda v, j=j, cBv=cBv: v.tensor_scalar(
                    ctxn[:, j * 128 + 64:j * 128 + 128], cBv,
                    scol[:, 3:4], None, ALU.mult),
                    reads=[ckB, "rB"], writes=[f"cn{j}"])

            def av15(hp=hp):
                p = 15 % 2
                eap = expT[:, p * 1024:(p + 1) * 1024]
                for j in range(4):
                    cA, ckA = cslot(0, j)
                    S_.op("pe", lambda t, j=j, eap=eap, cA=cA: t.matmul(
                        cA, eap[:, j * 128:j * 128 + 128],
                        v_aug[:, 15 * 1040 + 2 * hp * 65:15 * 1040 + 2 * hp * 65 + 65],
                        start=False, stop=True),
                        reads=[f"exp{p}", f"v15_{(2 * hp) // 8}"], writes=[ckA])
                    cB, ckB = cslot(1, j)
                    S_.op("pe", lambda t, j=j, eap=eap, cB=cB: t.matmul(
                        cB, eap[:, 512 + j * 128:512 + j * 128 + 128],
                        v_aug[:, 15 * 1040 + (2 * hp + 1) * 65:15 * 1040 + (2 * hp + 1) * 65 + 65],
                        start=False, stop=True),
                        reads=[f"exp{p}", f"v15_{(2 * hp + 1) // 8}"], writes=[ckB])
                    if j == 1:
                        norm_j(15, 0)
                        norm_j(15, 1)
                norm_j(15, 2)
                norm_j(15, 3)

            scores(0)
            scores(1)
            flush_fin()
            q_next()
            av(0)
            for i in range(2, 16):
                scores(i)
                av(i - 1)
            av15()

            def fin(hp=hp, cpo=cpo, pb=b % 2):
                for j in range(4):
                    tv = ps_t[:, j * 64:j * 64 + 64].bitcast(BF16)
                    S_.op("pe", lambda t, j=j, tv=tv: t.transpose(
                        tv, ctxn[:, j * 128:j * 128 + 128], id_sb[:, :]),
                        reads=[f"cn{j}", "id_sb"], writes=["ps6"])
                ev = ps_t[:, 0:256].bitcast(BF16)
                S_.op("dve", lambda v, hp=hp, ev=ev, cpo=cpo: v.tensor_copy(
                    ctx_fm[:, cpo + hp * 512:cpo + (hp + 1) * 512], ev),
                    reads=["ps6"],
                    writes=([f"cf{pb}_{hp}", f"tern{hp // 4}"]
                            + (["sgn0"] if (pb == 1 and hp < 4) else [])))

            fin_pending.append(fin)

        def b35_half(b, tt, hf, base):
            pb = b % 2
            xo = pb * 4096
            cpo = pb * 4096
            t0 = base + b * TB + tt * 128
            sl = tt % 2
            xti = xt[:, :]
            xkey = "xt0"
            gsl = g_sb[:, sl * 1024:(sl + 1) * 1024]
            gkey2 = f"g{sl}"
            if hf == 1:
                S_.op("pool", lambda g, t0=t0, xti=xti: g.dma_start(
                    xti, x[t0:t0 + 128, :]), writes=[xkey], dma=True)
            gp, gk = bank(7)
            for kc in range(8):
                S_.op("pe", lambda t, tt=tt, hf=hf, kc=kc, gp=gp, xo=xo: t.matmul(
                    gp,
                    xn_fm[:, xo + kc * 512 + tt * 128:xo + kc * 512 + tt * 128 + 128],
                    wT["f"][:, kc * 1024 + hf * 512:kc * 1024 + hf * 512 + 512],
                    start=(kc == 0), stop=False),
                    reads=[wT_keys["f"][kc], f"xnfm{pb}_{kc}"], writes=[gk])
            S_.op("pe", lambda t, hf=hf, gp=gp: t.matmul(
                gp, ones_bf[:, :], brow[:, 1024 + hf * 512:1024 + hf * 512 + 512],
                start=False, stop=True),
                reads=["ones_bf", "brow2"], writes=[gk])
            S_.op("act", lambda s, hf=hf, gsl=gsl, gp=gp: s.activation(
                gsl[:, hf * 512:(hf + 1) * 512], gp, AF.Silu,
                bias=f32(0.0), scale=f32(1.0)),
                reads=[gk], writes=[gkey2, "gat0", "gat1"])
            sp_, sk = bank(6)
            for kc in range(8):
                S_.op("pe", lambda t, tt=tt, hf=hf, kc=kc, sp_=sp_, cpo=cpo: t.matmul(
                    sp_,
                    ctx_fm[:, cpo + kc * 512 + tt * 128:cpo + kc * 512 + tt * 128 + 128],
                    wT["o"][:, kc * 1024 + hf * 512:kc * 1024 + hf * 512 + 512],
                    start=(kc == 0), stop=False),
                    reads=[wT_keys["o"][kc], f"cf{pb}_{kc}"], writes=[sk])
            S_.op("pe", lambda t, hf=hf, sp_=sp_: t.matmul(
                sp_, ones_bf[:, :], brow[:, hf * 512:hf * 512 + 512],
                start=False, stop=True),
                reads=["ones_bf", "brow0"], writes=[sk])
            S_.op("dve", lambda v, hf=hf, gsl=gsl, sp_=sp_: v.tensor_tensor(
                gsl[:, hf * 512:(hf + 1) * 512], sp_,
                gsl[:, hf * 512:(hf + 1) * 512], ALU.mult),
                reads=[sk, gkey2], writes=[gkey2])
            if hf == 1:
                S_.op("dve", lambda v, gsl=gsl, xti=xti: v.tensor_tensor(
                    gsl, gsl, xti, ALU.add),
                    reads=[gkey2, xkey], writes=[gkey2])
                S_.op("sp", lambda sy, t0=t0, gsl=gsl: sy.dma_start(
                    y[t0:t0 + 128, :], gsl),
                    reads=[gkey2], writes=[f"y{t0}"], dma=True)

        def b35_tile(b, tt, base):
            b35_half(b, tt, 0, base)
            b35_half(b, tt, 1, base)

        for sq in range(BPC):
            base = sq * S
            S_.op("pool", lambda g: g.dma_start(brow[:, 0:1024], bvr[:, :]),
                  writes=["brow0"], dma=True)
            for b in range(NBLK):
                if sq == 0:
                    for tt in range(NTIL):
                        ln_tile(b, tt, base)
                    if b == 0:
                        setup_late()
                xn_loads(b)
                if b >= 1:
                    kv_block(b - 1)
            kv_block(NBLK - 1)
            S_.op("pool", lambda g: g.dma_start(brow[:, 0:1024], bor[:, :]),
                  writes=["brow0"], dma=True)
            for b in range(NBLK):
                if b == 0:
                    xn_loads(0)
                    q_proj(0, 0)
                for hp in range(8):
                    if hp < 7:
                        qn = (lambda b=b, hp=hp: q_proj(b, hp + 1))
                    else:
                        qn = (lambda: None)
                    attn_pair(b, hp, qn)
                    if sq + 1 < BPC and 0 <= hp - 3 <= 3:
                        ln_finish(b, hp - 3, (sq + 1) * S)
                    if b >= 1:
                        b35_half(b - 1, hp // 2, hp % 2, base)
                    if sq + 1 < BPC and 0 <= hp - 2 <= 3:
                        ln_front(b, hp - 2, (sq + 1) * S)
                    if hp == 7 and b + 1 < NBLK:
                        xn_loads(b + 1)
                        q_proj(b + 1, 0)
                flush_fin()
            for tt in range(NTIL):
                b35_tile(NBLK - 1, tt, base)

        for l in range(NDMALANES):
            if S_.cnt[f"dsp{l}"]:
                S_.raw_wait("sp", f"dsp{l}", S_.cnt[f"dsp{l}"])

        def replay(eng, obj):
            for waits, fn, dom, inc in S_.q[eng]:
                for d, c in waits:
                    obj.wait_ge(sems[d], c)
                if fn is not None:
                    fn(obj).then_inc(sems[dom], inc)

        @block.tensor
        def _(t):
            replay("pe", t)

        @block.scalar
        def _(s):
            replay("act", s)

        @block.vector
        def _(v):
            replay("dve", v)

        @block.gpsimd
        def _(g):
            replay("pool", g)

        @block.sync
        def _(sy):
            replay("sp", sy)

    return nc


_NC_CACHE = None


def kernel(L0_attn_query, L0_attn_mask, L0_attn_q_w, L0_attn_q_b, L0_attn_k_w,
           L0_attn_k_b, L0_attn_v_w, L0_attn_v_b, L0_attn_out_w, L0_attn_out_b,
           L0_ffn_weight, L0_ffn_alpha, L0_ffn_bias):
    global _NC_CACHE
    if _NC_CACHE is None:
        _NC_CACHE = build_nc()
    nc = _NC_CACHE

    xq = np.ascontiguousarray(np.asarray(L0_attn_query, dtype=np.float32))
    trp = lambda w: np.ascontiguousarray(np.asarray(w, dtype=np.float32).T)
    row = lambda v: np.ascontiguousarray(
        np.asarray(v, dtype=np.float32).reshape(1, -1))
    colc = lambda v: np.ascontiguousarray(
        np.asarray(v, dtype=np.float32).reshape(8, 128).T)
    common = {
        "wqT": trp(L0_attn_q_w), "wkT": trp(L0_attn_k_w), "wvT": trp(L0_attn_v_w),
        "woT": trp(L0_attn_out_w), "wfT": trp(L0_ffn_weight),
        "bvr": row(L0_attn_v_b), "bor": row(L0_attn_out_b), "bfr": row(L0_ffn_bias),
        "qbc": colc(L0_attn_q_b), "kbc": colc(L0_attn_k_b),
        "alv": np.ascontiguousarray(np.broadcast_to(
            np.asarray(L0_ffn_alpha, dtype=np.float32).reshape(1, 1), (128, 1))),
        "ident": np.eye(128, dtype=np.float32).astype(ml_dtypes.bfloat16),
    }
    in_maps = []
    for c in range(NCORES):
        m = dict(common)
        m["x"] = np.ascontiguousarray(
            xq[c * BPC:(c + 1) * BPC].reshape(BPC * S, D))
        in_maps.append(m)
    res = run_bass_kernel_spmd(nc, in_maps, core_ids=list(range(NCORES)))
    out = np.concatenate(
        [np.asarray(r["y"]).reshape(BPC, S, D) for r in res.results], axis=0)
    return out.astype(np.float32, copy=False)
